# version 11
# baseline (speedup 1.0000x reference)
import contextlib
import numpy as np
import concourse.bass as bass
import concourse.mybir as mybir
from concourse.bass_utils import run_bass_kernel_spmd

F32 = mybir.dt.float32
F32R = mybir.dt.float32r
I32 = mybir.dt.int32
AF = mybir.ActivationFunctionType
OP = mybir.AluOpType

D = 2048
NTOK = 1024
NT = 8
MC = 16
H = 16
NDB = 8
DBW = 256
INTER = 5504
ICR = 43
ICC = 6
EPS_LN = 1e-5
EPS_AT = 1e-6
SLAB_SIZES = [5, 5, 5, 5, 5, 5, 5, 5, 3]
SLABS = []
_i0 = 0
for _s in SLAB_SIZES:
    SLABS.append(list(range(_i0, _i0 + _s)))
    _i0 += _s
INV_D = 1.0 / D

KV_BANK = [hh // 3 for hh in range(H)]
KV_OFF = [(hh % 3) * 130 for hh in range(H)]
_first_h = {}
_last_h = {}
for _hh in range(H):
    _b = KV_BANK[_hh]
    _first_h.setdefault(_b, _hh)
    _last_h[_b] = _hh

GTILES = [(m, j, hf) for m in range(3) for j in range(ICC) for hf in range(2)]
NGT = len(GTILES)
WEAVE_AFTER_HEAD = {2 + i: 24 + i for i in range(12)}


def build():
    nc = bass.Bass("TRN2", target_bir_lowering=False, debug=False,
                   detect_race_conditions=False)

    x_h = nc.dram_tensor("x", [NTOK, D], F32, kind="ExternalInput").ap()
    wq_h = nc.dram_tensor("wq", [H, MC, 128, 128], F32R, kind="ExternalInput").ap()
    wk_h = nc.dram_tensor("wk", [NDB, MC, 128, DBW], F32R, kind="ExternalInput").ap()
    wv_h = nc.dram_tensor("wv", [NDB, MC, 128, DBW], F32R, kind="ExternalInput").ap()
    wo_h = nc.dram_tensor("wo", [NDB, MC, 128, DBW], F32R, kind="ExternalInput").ap()
    bqc_h = nc.dram_tensor("bqc", [128, H], F32, kind="ExternalInput").ap()
    bkv_h = nc.dram_tensor("bkv", [2, 2 * D], F32R, kind="ExternalInput").ap()
    bo_h = nc.dram_tensor("bo", [2, D], F32R, kind="ExternalInput").ap()
    gidx_h = nc.dram_tensor("gidx", [ICC, 128, D], I32, kind="ExternalInput").ap()
    uidx_h = nc.dram_tensor("uidx", [ICC, 128, D], I32, kind="ExternalInput").ap()
    didx_h = nc.dram_tensor("didx", [ICC, 128, D], I32, kind="ExternalInput").ap()
    lut_h = nc.dram_tensor("lut", [16], F32, kind="ExternalInput").ap()
    dcol_h = nc.dram_tensor("dcol", [128, NT], F32, kind="ExternalInput").ap()
    id_h = nc.dram_tensor("ident", [128, 128], F32, kind="ExternalInput").ap()
    ones_h = nc.dram_tensor("onesm", [128, 512], F32R, kind="ExternalInput").ap()
    y_h = nc.dram_tensor("y", [NTOK, D], F32, kind="ExternalOutput").ap()

    sh_h = [nc.dram_tensor(n, [ICC, 128, D], F32R) for n in ("sh_g", "sh_u", "sh_d")]
    bl_h = [nc.dram_tensor(n, [8 * ICC, 128, D], F32R, addr_space="Shared")
            for n in ("bl_g", "bl_u", "bl_d")]
    kvin_h = nc.dram_tensor("kv_in", [128, H * 130], F32)
    kvout_h = nc.dram_tensor("kv_out", [128, H * 130], F32, addr_space="Shared")

    marks = {}
    state = {"real": False}
    waits_seen = []

    class Ctr:
        def __init__(self):
            self.v = 0

        def inc(self, n=1):
            self.v += n
            return self.v

    def setmark(key, v):
        if not state["real"]:
            assert key not in marks, f"dup mark {key}"
            marks[key] = v
        else:
            assert marks[key] == v, (key, marks[key], v)

    def emit(real, ctx):
        state["real"] = real
        from collections import defaultdict
        ctrs = defaultdict(Ctr)

        def G(name):
            return ctx.get(name)

        def I(sem, n, key, call):
            if real:
                inst = call()
                if n:
                    inst.then_inc(ctx[sem], n)
            if n:
                v = ctrs[sem].inc(n)
                if key is not None:
                    setmark(key, (sem, v))

        def wt(eng, key):
            if not real:
                waits_seen.append(key)
                return
            nm, v = marks[key]
            eng.wait_ge(ctx[nm], v)

        def mark_here(key, sem):
            setmark(key, (sem, ctrs[sem].v))

        def vw(v):
            if real:
                v.wait_ge(ctx["sVE"], ctrs["sVE"].v)

        def sw(s):
            if real:
                s.wait_ge(ctx["sSE"], ctrs["sSE"].v)

        def sync_p0(s):
            I("sC", 16, "ld_ident", lambda: s.dma_start(out=G("idents")[:], in_=id_h))
            I("sC", 16, "ld_lut", lambda: s.dma_start(
                out=G("luts")[:], in_=lut_h[None, :].broadcast_to([128, 16])))
            I("sC", 16, "ld_dcol", lambda: s.dma_start(out=G("dcols")[:], in_=dcol_h))
            I("sC", 16, "ld_bqc", lambda: s.dma_start(out=G("bqcs")[:], in_=bqc_h))
            I("sC", 16, "ld_bkv", lambda: s.dma_start(out=G("bkvs")[:], in_=bkv_h))
            for t in range(2):
                I(f"sX{t % 2}", 16, f"ld_x{t}", lambda t=t: s.dma_start(
                    out=G(f"xt{t}")[:], in_=x_h[t * 128:(t + 1) * 128, :]))
            for k in range(2):
                m, j, hf = GTILES[k]
                I(f"sGI{k % 2}", 16, f"ld_gi{k}", lambda k=k, m=m, j=j, hf=hf: s.dma_start(
                    out=G(f"gi{k % 2}")[:],
                    in_=[gidx_h, uidx_h, didx_h][m][j, :, hf * 1024:(hf + 1) * 1024]))
            I("sC", 16, "ld_ones", lambda: s.dma_start(out=G("ones")[:], in_=ones_h[0:2, :]))
            I("sC", 16, "ld_onec", lambda: s.dma_start(out=G("onec")[:], in_=ones_h[:, 0:2], allow_slow_non_contiguous=True))
            I("sWA", 16, "ld_wv0", lambda: s.dma_start(
                out=G("wva")[:].rearrange("p (k f) -> p k f", k=MC), in_=wv_h[0].rearrange("k p f -> p k f")))
            for t in range(2, NT):
                wt(s, f"pe_tp_end{t - 2}")
                wt(s, f"gp_yinit{t - 2}")
                I(f"sX{t % 2}", 16, f"ld_x{t}", lambda t=t: s.dma_start(
                    out=G(f"xt{t % 2}")[:], in_=x_h[t * 128:(t + 1) * 128, :]))

        def sync_p3(s):
            order = []
            gk = 2
            for db in range(NDB):
                order.append(("wk", db))
                order.append(("gi", gk)); gk += 1
                if db < NDB - 1:
                    order.append(("wv", db + 1))
                    order.append(("gi", gk)); gk += 1
            while gk < 24:
                order.append(("gi", gk)); gk += 1
            for kind, idx in order:
                if kind == "gi":
                    k = idx
                    m, j, hf = GTILES[k]
                    wt(s, f"ve_g{k - 2}")
                    I(f"sGI{k % 2}", 16, f"ld_gi{k}", lambda k=k, m=m, j=j, hf=hf: s.dma_start(
                        out=G(f"gi{k % 2}")[:],
                        in_=[gidx_h, uidx_h, didx_h][m][j, :, hf * 1024:(hf + 1) * 1024]))
                elif kind == "wv":
                    db = idx
                    wt(s, f"pe_v_end{db - 1}")
                    I("sWA", 16, f"ld_wv{db}", lambda db=db: s.dma_start(
                        out=G("wva")[:].rearrange("p (k f) -> p k f", k=MC), in_=wv_h[db].rearrange("k p f -> p k f")))
                else:
                    db = idx
                    if db >= 1:
                        wt(s, f"pe_k_end{db - 1}")
                    I("sWB", 16, f"ld_wk{db}", lambda db=db: s.dma_start(
                        out=G("wvb")[:].rearrange("p (k f) -> p k f", k=MC), in_=wk_h[db].rearrange("k p f -> p k f")))

        def sync_p4(s):
            order = []
            gk = 24
            for h in range(H):
                order.append(("wq", h))
                if h >= 2 and gk < 36:
                    order.append(("gi", gk)); gk += 1
            while gk < 36:
                order.append(("gi", gk)); gk += 1
            for kind, idx in order:
                if kind == "wq":
                    h = idx
                    if h < 2:
                        wt(s, "pe_p3_end")
                    else:
                        wt(s, f"pe_q_end{h - 2}")
                    I(f"sWQ{h % 2}", 16, f"ld_wq{h}", lambda h=h: s.dma_start(
                        out=G(f"wq{h % 2}")[:].rearrange("p (k f) -> p k f", k=MC), in_=wq_h[h].rearrange("k p f -> p k f")))
                else:
                    k = idx
                    m, j, hf = GTILES[k]
                    wt(s, f"ve_g{k - 2}")
                    I(f"sGI{k % 2}", 16, f"ld_gi{k}", lambda k=k, m=m, j=j, hf=hf: s.dma_start(
                        out=G(f"gi{k % 2}")[:],
                        in_=[gidx_h, uidx_h, didx_h][m][j, :, hf * 1024:(hf + 1) * 1024]))

        def sync_p5(s):
            wt(s, "pe_attn_end")
            I("sWO", 16, "ld_bo", lambda: s.dma_start(out=G("bos")[:], in_=bo_h))
            for ob in range(NDB):
                if ob < 1:
                    wt(s, "pe_attn_end")
                else:
                    wt(s, f"pe_wo_end{ob - 1}")
                I("sWO", 16, f"ld_wo{ob}", lambda ob=ob: s.dma_start(
                    out=G("wob0")[:].rearrange("p (k f) -> p k f", k=MC), in_=wo_h[ob].rearrange("k p f -> p k f")))

        def sync_p6(s):
            first = {"g": True, "u": True}
            for si, slab in enumerate(SLABS):
                for ic in slab:
                    for mat, mi, cck in (("g", 0, "cc_ag_g"), ("u", 1, "cc_ag_u")):
                        if first[mat]:
                            wt(s, cck)
                            wt(s, "pe_wo_end7")
                            first[mat] = False
                        if ic >= 2:
                            wt(s, f"pe_{mat}end{ic - 2}")
                        I(f"s{mat.upper()}{ic % 2}", 16, f"ld_{mat}{ic}", lambda mat=mat, mi=mi, ic=ic: s.dma_start(
                            out=G(f"{mat}tp{ic % 2}")[:], in_=bl_h[mi].ap()[ic]))
                for jj, ic in enumerate(slab):
                    if si == 0 and jj == 0:
                        wt(s, "cc_ag_d")
                        wt(s, "pe_wo_end7")
                    if si >= 1 and jj == 0:
                        wt(s, f"pe_dn_end{si - 1}")
                    I("sDT", 16, f"ld_dt{si}_{jj}", lambda si=si, jj=jj, ic=ic: s.dma_start(
                        out=G("dts")[:, jj * D:(jj + 1) * D], in_=bl_h[2].ap()[ic]))

        def scalar_p1(s):
            wt(s, "ld_onec")
            def stat_pair(t):
                wt(s, f"ld_x{t}")
                I("sSE", 1, f"se_ssq{t}", lambda t=t: s.activation(
                    G("sqs")[:], G(f"xt{t % 2}")[:], AF.Square,
                    accum_out=G("stats")[:, 8 + t:9 + t]))
                wt(s, f"ve_var{t}")
                if t == 0:
                    wt(s, "ve_eps")
                I("sSE", 1, f"se_sd{t}", lambda t=t: s.activation(
                    G("stats")[:, 40 + t:41 + t], G("stats")[:, 41 + t:42 + t],
                    AF.Sqrt, bias=G("epsc")[:, 0:1]))
            stat_pair(0)
            stat_pair(1)
            for t in range(NT):
                for mc_ in range(MC):
                    wt(s, f"pe_tp{t}_{mc_}")
                    I("sSE", 1, f"se_ev{t}_{mc_}", lambda t=t, mc_=mc_: s.copy(
                        G("xnT")[:, mc_ * NTOK + t * 128: mc_ * NTOK + (t + 1) * 128],
                        G(f"pj{(t * MC + mc_) % 2}")[:, 0:128]))
                if t + 2 < NT:
                    stat_pair(t + 2)

        def scalar_p3(s):
            for db in range(NDB):
                for t in range(NT):
                    wt(s, f"pe_v{db}_{t}")
                    if db >= 1:
                        wt(s, f"pe_kv{db - 1}_{t}")
                    I("sSE", 1, f"se_vev{db}_{t}", lambda db=db, t=t: s.copy(
                        G("vall")[:, t * DBW:(t + 1) * DBW],
                        G(f"pj{t % 2}")[:, 0:DBW]))
                for t in range(NT):
                    wt(s, f"pe_kp{db}_{t}")
                    if t >= 2:
                        wt(s, f"pe_kv{db}_{t - 2}")
                    elif db >= 1:
                        wt(s, f"pe_kv{db - 1}_{t + 6}")
                    I("sSE", 1, f"se_kev{db}_{t}", lambda db=db, t=t: s.activation(
                        G(f"kt{t % 2}")[:], G(f"pj{t % 2}")[:, 0:DBW],
                        AF.Relu, scale=G("dcols")[:, t:t + 1]))
            wt(s, "pe_p3_end")
            for b in range(6):
                size = 390 if b < 5 else 130
                I("sSE", 1, f"se_kvev{b}", lambda b=b, size=size: s.copy(
                    G("kvb")[:, b * 390: b * 390 + size],
                    G(f"kvp{b}")[:, 0:size]))
            sw(s)
            I("sKV", 16, "se_kvin", lambda: s.dma_start(out=kvin_h.ap(), in_=G("kvb")[:]))
            wt(s, "cc_ar")
            I("sKV", 16, "se_kvout", lambda: s.dma_start(out=G("kvb")[:], in_=kvout_h.ap()))

        def scalar_p4(s):
            for h in range(H):
                for nb in range(2):
                    wt(s, f"pe_q{h}_{nb}")
                    if h >= 1:
                        wt(s, f"pe_at{h - 1}_7")
                    I("sSE", 1, f"se_qev{h}_{nb}", lambda h=h, nb=nb: s.activation(
                        G("qh0")[:, nb * 512:(nb + 1) * 512],
                        G(f"qp{nb}")[:], AF.Relu, bias=G("bqcs")[:, h:h + 1]))
                for t in range(NT):
                    i = h * NT + t
                    wt(s, f"ve_z{h}_{t}")
                    if h >= 4:
                        wt(s, f"pe_troc{h // 4 - 1}_{t * 4 + h % 4}")
                    I("sSE", 1, f"se_ocs{h}_{t}", lambda h=h, t=t, i=i: s.activation(
                        G("ocp")[:, t * 512 + (h % 4) * 128: t * 512 + (h % 4) * 128 + 128],
                        G(f"at{i % 3}")[:, 0:128],
                        AF.Copy, scale=G("zs")[:, 2 * (i % 2) + 1: 2 * (i % 2) + 2]))
                if h % 4 == 3:
                    g = h // 4
                    for j in range(32):
                        tt, hc = j // 4, j % 4
                        wt(s, f"pe_troc{g}_{j}")
                        I("sSE", 1, f"se_ocT{g}_{j}", lambda g=g, j=j, tt=tt, hc=hc: s.copy(
                            G("ocT")[:, (4 * g + hc) * NTOK + tt * 128:
                                     (4 * g + hc) * NTOK + (tt + 1) * 128],
                            G(f"tr{j % 2}")[:, 0:128]))

        def scalar_p5(s):
            for ob in range(NDB):
                for t in range(NT):
                    i = ob * NT + t
                    wt(s, f"pe_wo{ob}_{t}")
                    if i >= 2:
                        wt(s, f"gp_ywo{i - 2}")
                    I("sSE", 1, f"se_woev{ob}_{t}", lambda ob=ob, t=t, i=i: s.copy(
                        G(f"ytmp{i % 2}")[:, 0:DBW], G(f"wo{i % 2}")[:, 0:DBW]))

        def scalar_p6(s):
            dnj = 0
            for si, slab in enumerate(SLABS):
                for ic in slab:
                    for nb in range(2):
                        wt(s, f"pe_gate{ic}_{nb}")
                        if ic >= 2:
                            wt(s, f"ve_h{ic - 2}")
                        I("sSE", 1, f"se_silu{ic}_{nb}", lambda ic=ic, nb=nb: s.activation(
                            G("gst")[:, (ic % 2) * 1024 + nb * 512:
                                     (ic % 2) * 1024 + (nb + 1) * 512],
                            G(f"fp{4 * (ic % 2) + nb}")[:], AF.Silu))
                for ob in range(4):
                    for ot in range(NT):
                        wt(s, f"pe_dn{si}_{ob}_{ot}")
                        if dnj < 2:
                            wt(s, f"gp_ywo{62 + dnj}")
                        else:
                            wt(s, f"gp_ydn{dnj - 2}")
                        I("sSE", 1, f"se_dnev{si}_{ob}_{ot}",
                          lambda dnj=dnj, ot=ot: s.copy(
                              G(f"ytmp{dnj % 2}")[:], G(f"fp{ot}")[:]))
                        dnj += 1

        def gather_tile(v, k):
            wt(v, f"ld_gi{k}")
            if k >= 2:
                wt(v, f"gp_gw{k - 2}")
            I("sVE", 0, None, lambda k=k: v.tensor_scalar(
                G(f"gw{k % 2}")[:], G(f"gi{k % 2}")[:], 0.0, G("luts")[:, 0:1],
                OP.is_equal, OP.mult))
            for val in range(1, 16):
                I("sVE", 0, None, lambda k=k, val=val: v.tensor_scalar(
                    G("gtm")[:], G(f"gi{k % 2}")[:], float(val),
                    G("luts")[:, val:val + 1], OP.is_equal, OP.mult))
                last = val == 15
                I("sVE", 1 if last else 0, f"ve_g{k}" if last else None,
                  lambda k=k: v.tensor_add(G(f"gw{k % 2}")[:], G(f"gw{k % 2}")[:],
                                           G("gtm")[:]))

        def vector_p1(v):
            wt(v, "ld_onec")
            I("sVE", 1, "ve_eps", lambda: v.memset(G("epsc")[:], EPS_LN))
            for t in range(NT):
                wt(v, f"ld_x{t}")
                I("sVE", 1, None, lambda t=t: v.reduce_sum(
                    G("stats")[:, t:t + 1], G(f"xt{t % 2}")[:], mybir.AxisListType.X))
                wt(v, f"se_ssq{t}")
                vw(v)
                I("sVE", 1, None, lambda t=t: v.tensor_single_scalar(
                    G("stats")[:, 16 + t:17 + t], G("stats")[:, t:t + 1], INV_D, OP.mult))
                I("sVE", 1, None, lambda t=t: v.tensor_single_scalar(
                    G("stats")[:, 24 + t:25 + t], G("stats")[:, 8 + t:9 + t], INV_D, OP.mult))
                vw(v)
                I("sVE", 1, None, lambda t=t: v.tensor_tensor(
                    G("stats")[:, 33 + t:34 + t], G("stats")[:, 16 + t:17 + t],
                    G("stats")[:, 16 + t:17 + t], OP.mult))
                vw(v)
                I("sVE", 1, f"ve_var{t}", lambda t=t: v.tensor_tensor(
                    G("stats")[:, 41 + t:42 + t], G("stats")[:, 24 + t:25 + t],
                    G("stats")[:, 33 + t:34 + t], OP.subtract))
                wt(v, f"se_sd{t}")
                wt(v, f"gp_yinit{t}")
                I("sVE", 1, None, lambda t=t: v.reciprocal(
                    G("stats")[:, 48 + t:49 + t], G("stats")[:, 40 + t:41 + t]))
                vw(v)
                I("sVE", 1, f"ve_ln{t}", lambda t=t: v.tensor_scalar(
                    G(f"xt{t % 2}")[:], G(f"xt{t % 2}")[:],
                    G("stats")[:, 16 + t:17 + t], G("stats")[:, 48 + t:49 + t],
                    OP.subtract, OP.mult))

        def vector_p2(v):
            for k in range(24):
                gather_tile(v, k)
            wt(v, "se_kvout")
            I("sVE", 1, "ve_kvr", lambda: v.tensor_copy(G("kvr")[:], G("kvb")[:]))

        def vector_p4(v):
            for h in range(H):
                for t in range(NT):
                    i = h * NT + t
                    zi = 2 * (i % 2)
                    wt(v, f"pe_at{h}_{t}")
                    if i >= 2:
                        wt(v, f"se_ocs{(i - 2) // NT}_{(i - 2) % NT}")
                    I("sVE", 1, None, lambda i=i, t=t, zi=zi: v.tensor_tensor(
                        G("zs")[:, zi:zi + 1], G(f"at{i % 3}")[:, 128:129],
                        G("dcols")[:, t:t + 1], OP.mult))
                    vw(v)
                    I("sVE", 1, None, lambda zi=zi: v.tensor_single_scalar(
                        G("zs")[:, zi + 1:zi + 2], G("zs")[:, zi:zi + 1], EPS_AT, OP.add))
                    vw(v)
                    I("sVE", 1, None, lambda zi=zi: v.reciprocal(
                        G("zs")[:, zi:zi + 1], G("zs")[:, zi + 1:zi + 2]))
                    vw(v)
                    I("sVE", 1, f"ve_z{h}_{t}", lambda t=t, zi=zi: v.tensor_tensor(
                        G("zs")[:, zi + 1:zi + 2], G("zs")[:, zi:zi + 1],
                        G("dcols")[:, t:t + 1], OP.mult))
                if h in WEAVE_AFTER_HEAD:
                    gather_tile(v, WEAVE_AFTER_HEAD[h])

        def vector_p6(v):
            for si, slab in enumerate(SLABS):
                for jj, ic in enumerate(slab):
                    if jj == 0 and si >= 1:
                        wt(v, f"pe_dn_end{si - 1}")
                    for nb in range(2):
                        wt(v, f"pe_up{ic}_{nb}")
                        wt(v, f"se_silu{ic}_{nb}")
                        last = nb == 1
                        I("sVE", 1 if last else 0, f"ve_h{ic}" if last else None,
                          lambda ic=ic, jj=jj, nb=nb: v.tensor_tensor(
                              G("hsl")[:, jj * 1024 + nb * 512: jj * 1024 + (nb + 1) * 512],
                              G("gst")[:, (ic % 2) * 1024 + nb * 512:
                                       (ic % 2) * 1024 + (nb + 1) * 512],
                              G(f"fp{4 * (ic % 2) + 2 + nb}")[:], OP.mult))

        def tensor_p1(t_):
            wt(t_, "ld_onec")
            for t in range(NT):
                wt(t_, f"ve_ln{t}")
                for mc_ in range(MC):
                    slot = t * MC + mc_
                    if slot >= 2:
                        wt(t_, f"se_ev{(slot - 2) // MC}_{(slot - 2) % MC}")
                    I("sPE", 1, f"pe_tp{t}_{mc_}", lambda t=t, mc_=mc_, slot=slot: t_.transpose(
                        G(f"pj{slot % 2}")[:, 0:128],
                        G(f"xt{t % 2}")[:, mc_ * 128:(mc_ + 1) * 128],
                        G("idents")[:]))
                mark_here(f"pe_tp_end{t}", "sPE")

        def proj_tile_tm(t_, pjname, xt_t, rhsname, bias_off, key):
            for mc_ in range(MC):
                I("sPE", 0, None, lambda mc_=mc_: t_.matmul(
                    G(pjname)[:, 0:DBW],
                    G("xnT")[:, mc_ * NTOK + xt_t * 128: mc_ * NTOK + (xt_t + 1) * 128],
                    G(rhsname)[:, mc_ * DBW:(mc_ + 1) * DBW],
                    start=(mc_ == 0), stop=False))
            I("sPE", 1, key, lambda: t_.matmul(
                G(pjname)[:, 0:DBW], G("ones")[0:2, 0:128],
                G("bkvs")[0:2, bias_off:bias_off + DBW],
                start=False, stop=True))

        def tensor_p3(t_):
            wt(t_, "ld_onec")
            wt(t_, f"se_ev{NT - 1}_{MC - 1}")
            kv_started = set()
            kv_count = {}
            total_per_bank = {}
            for hh in range(H):
                total_per_bank[KV_BANK[hh]] = total_per_bank.get(KV_BANK[hh], 0) + NT * 2
            for db in range(NDB):
                wt(t_, f"ld_wv{db}")
                for t in range(NT):
                    if t >= 2:
                        wt(t_, f"se_vev{db}_{t - 2}")
                    elif db >= 1:
                        wt(t_, f"se_kev{db - 1}_{6 + t}")
                    proj_tile_tm(t_, f"pj{t % 2}", t, "wva", D + db * DBW, f"pe_v{db}_{t}")
                mark_here(f"pe_v_end{db}", "sPE")
                wt(t_, f"ld_wk{db}")
                for t in range(NT):
                    if t >= 2:
                        wt(t_, f"se_kev{db}_{t - 2}")
                    else:
                        wt(t_, f"se_vev{db}_{6 + t}")
                    proj_tile_tm(t_, f"pj{t % 2}", t, "wvb", db * DBW, f"pe_kp{db}_{t}")
                    wt(t_, f"se_kev{db}_{t}")
                    for hl in range(2):
                        hh = db * 2 + hl
                        b = KV_BANK[hh]
                        off = KV_OFF[hh]
                        start = b not in kv_started
                        kv_started.add(b)
                        kv_count[b] = kv_count.get(b, 0) + 2
                        stop = kv_count[b] == total_per_bank[b]
                        I("sPE", 0, None, lambda b=b, off=off, t=t, hl=hl, start=start:
                          t_.matmul(G(f"kvp{b}")[:, off:off + 128],
                                    G(f"kt{t % 2}")[:, hl * 128:(hl + 1) * 128],
                                    G("vall")[:, t * DBW + hl * 128: t * DBW + (hl + 1) * 128],
                                    start=start, stop=False))
                        I("sPE", 1, None, lambda b=b, off=off, t=t, hl=hl, stop=stop:
                          t_.matmul(G(f"kvp{b}")[:, off + 128:off + 130],
                                    G(f"kt{t % 2}")[:, hl * 128:(hl + 1) * 128],
                                    G("onec")[:, 0:2],
                                    start=False, stop=stop))
                    mark_here(f"pe_kv{db}_{t}", "sPE")
                mark_here(f"pe_k_end{db}", "sPE")
            mark_here("pe_p3_end", "sPE")

        def tensor_p4(t_):
            for b in range(6):
                wt(t_, f"se_kvev{b}")
            wt(t_, "ve_kvr")
            for h in range(H):
                wt(t_, f"ld_wq{h}")
                for nb in range(2):
                    if h >= 1:
                        wt(t_, f"se_qev{h - 1}_{nb}")
                    for mc_ in range(MC):
                        last = mc_ == MC - 1
                        I("sPE", 1 if last else 0, f"pe_q{h}_{nb}" if last else None,
                          lambda h=h, nb=nb, mc_=mc_, last=last: t_.matmul(
                              G(f"qp{nb}")[:],
                              G(f"wq{h % 2}")[:, mc_ * 128:(mc_ + 1) * 128],
                              G("xnT")[:, mc_ * NTOK + nb * 512: mc_ * NTOK + nb * 512 + 512],
                              start=(mc_ == 0), stop=last))
                mark_here(f"pe_q_end{h}", "sPE")
                wt(t_, f"se_qev{h}_0")
                wt(t_, f"se_qev{h}_1")
                for t in range(NT):
                    i = h * NT + t
                    if i >= 3:
                        wt(t_, f"ve_z{(i - 3) // NT}_{(i - 3) % NT}")
                        wt(t_, f"se_ocs{(i - 3) // NT}_{(i - 3) % NT}")
                    I("sPE", 1, f"pe_at{h}_{t}", lambda h=h, t=t, i=i: t_.matmul(
                        G(f"at{i % 3}")[:, 0:130],
                        G("qh0")[:, t * 128:(t + 1) * 128],
                        G("kvr")[:, h * 130:(h + 1) * 130],
                        start=True, stop=True))
                if h % 4 == 3:
                    g = h // 4
                    for j in range(32):
                        tt, hc = j // 4, j % 4
                        wt(t_, f"se_ocs{4 * g + hc}_{tt}")
                        if j >= 2:
                            wt(t_, f"se_ocT{g}_{j - 2}")
                        elif g >= 1:
                            wt(t_, f"se_ocT{g - 1}_{30 + j}")
                        I("sPE", 1, f"pe_troc{g}_{j}", lambda g=g, j=j, tt=tt, hc=hc: t_.transpose(
                            G(f"tr{j % 2}")[:, 0:128],
                            G("ocp")[:, tt * 512 + hc * 128: tt * 512 + (hc + 1) * 128],
                            G("idents")[:]))
            mark_here("pe_attn_end", "sPE")

        def tensor_p5(t_):
            wt(t_, "se_ocT3_31")
            wt(t_, "ve_z15_7")
            for ob in range(NDB):
                wt(t_, f"ld_wo{ob}")
                for t in range(NT):
                    i = ob * NT + t
                    if i >= 2:
                        wt(t_, f"se_woev{(i - 2) // NT}_{(i - 2) % NT}")
                    for mc_ in range(MC):
                        I("sPE", 0, None, lambda t=t, mc_=mc_, i=i, ob=ob: t_.matmul(
                            G(f"wo{i % 2}")[:, 0:DBW],
                            G("ocT")[:, mc_ * NTOK + t * 128: mc_ * NTOK + (t + 1) * 128],
                            G("wob0")[:, mc_ * DBW:(mc_ + 1) * DBW],
                            start=(mc_ == 0), stop=False))
                    I("sPE", 1, f"pe_wo{ob}_{t}", lambda ob=ob, i=i: t_.matmul(
                        G(f"wo{i % 2}")[:, 0:DBW], G("ones")[0:2, 0:128],
                        G("bos")[0:2, ob * DBW:(ob + 1) * DBW],
                        start=False, stop=True))
                mark_here(f"pe_wo_end{ob}", "sPE")

        def tensor_p6(t_):
            for si, slab in enumerate(SLABS):
                for jj, ic in enumerate(slab):
                    for mat in ("g", "u"):
                        wt(t_, f"ld_{mat}{ic}")
                        if si == 0 and jj == 0 and mat == "g":
                            wt(t_, "se_woev7_7")
                        if si >= 1 and jj == 0 and mat == "g":
                            for ot in range(NT):
                                wt(t_, f"se_dnev{si - 1}_3_{ot}")
                        if ic >= 2:
                            if mat == "g":
                                wt(t_, f"se_silu{ic - 2}_0")
                                wt(t_, f"se_silu{ic - 2}_1")
                            else:
                                wt(t_, f"ve_h{ic - 2}")
                        for nb in range(2):
                            bank = (4 * (ic % 2)) + (0 if mat == "g" else 2) + nb
                            mmkey = (f"pe_gate{ic}_{nb}" if mat == "g"
                                     else f"pe_up{ic}_{nb}")
                            for mc_ in range(MC):
                                last = mc_ == MC - 1
                                I("sPE", 1 if last else 0, mmkey if last else None,
                                  lambda mat=mat, ic=ic, nb=nb, mc_=mc_, bank=bank, last=last:
                                  t_.matmul(G(f"fp{bank}")[:],
                                            G(f"{mat}tp{ic % 2}")[:, mc_ * 128:(mc_ + 1) * 128],
                                            G("xnT")[:, mc_ * NTOK + nb * 512:
                                                     mc_ * NTOK + nb * 512 + 512],
                                            start=(mc_ == 0), stop=last))
                        mark_here(f"pe_{mat}end{ic}", "sPE")
                for ic2 in slab:
                    wt(t_, f"ve_h{ic2}")
                wt(t_, f"ld_dt{si}_{len(slab) - 1}")
                wt(t_, f"se_silu{slab[-1]}_0")
                wt(t_, f"se_silu{slab[-1]}_1")
                for ob in range(4):
                    for ot in range(NT):
                        if ob >= 1:
                            wt(t_, f"se_dnev{si}_{ob - 1}_{ot}")
                        elif si >= 1:
                            wt(t_, f"se_dnev{si - 1}_3_{ot}")
                    for jj, ic in enumerate(slab):
                        for ot in range(NT):
                            last = jj == len(slab) - 1
                            I("sPE", 1 if last else 0,
                              f"pe_dn{si}_{ob}_{ot}" if last else None,
                              lambda jj=jj, ob=ob, ot=ot, last=last: t_.matmul(
                                  G(f"fp{ot}")[:],
                                  G("hsl")[:, jj * 1024 + ot * 128: jj * 1024 + (ot + 1) * 128],
                                  G("dts")[:, jj * D + ob * 512: jj * D + ob * 512 + 512],
                                  start=(jj == 0), stop=last))
                mark_here(f"pe_dn_end{si}", "sPE")

        def gpsimd_all(g):
            for t in range(NT):
                wt(g, f"ld_x{t}")
                I("sYI", 16, f"gp_yinit{t}", lambda t=t: g.dma_start(
                    out=y_h[t * 128:(t + 1) * 128, :], in_=G(f"xt{t % 2}")[:]))
            for k in range(NGT):
                m, j, hf = GTILES[k]
                wt(g, f"ve_g{k}")
                I(f"sGW{k % 2}", 16, f"gp_gw{k}", lambda k=k, m=m, j=j, hf=hf: g.dma_start(
                    out=sh_h[m].ap()[j, :, hf * 1024:(hf + 1) * 1024],
                    in_=G(f"gw{k % 2}")[:]))
                if k == 11:
                    I("sCG", 1, "cc_ag_g", lambda: g.collective_compute(
                        "AllGather", OP.bypass, replica_groups=[list(range(8))],
                        ins=[sh_h[0].ap().opt()], outs=[bl_h[0].ap().opt()]))
                    wt(g, "se_kvin")
                    I("sCR", 1, "cc_ar", lambda: g.collective_compute(
                        "AllReduce", OP.add, replica_groups=[list(range(8))],
                        ins=[kvin_h.ap().opt()], outs=[kvout_h.ap().opt()]))
                if k == 23:
                    I("sCU", 1, "cc_ag_u", lambda: g.collective_compute(
                        "AllGather", OP.bypass, replica_groups=[list(range(8))],
                        ins=[sh_h[1].ap().opt()], outs=[bl_h[1].ap().opt()]))
                if k == 35:
                    I("sCD", 1, "cc_ag_d", lambda: g.collective_compute(
                        "AllGather", OP.bypass, replica_groups=[list(range(8))],
                        ins=[sh_h[2].ap().opt()], outs=[bl_h[2].ap().opt()]))
            for ob in range(NDB):
                for t in range(NT):
                    i = ob * NT + t
                    wt(g, f"se_woev{ob}_{t}")
                    I(f"sY{i % 2}", 16, f"gp_ywo{i}", lambda ob=ob, t=t, i=i: g.dma_start(
                        out=y_h[t * 128:(t + 1) * 128, ob * DBW:(ob + 1) * DBW],
                        in_=G(f"ytmp{i % 2}")[:, 0:DBW], accum_op=OP.add))
            dnj = 0
            for si in range(len(SLABS)):
                for ob in range(4):
                    for ot in range(NT):
                        wt(g, f"se_dnev{si}_{ob}_{ot}")
                        I(f"sY{dnj % 2}", 16, f"gp_ydn{dnj}", lambda ob=ob, ot=ot, dnj=dnj:
                          g.dma_start(out=y_h[ot * 128:(ot + 1) * 128,
                                              ob * 512:ob * 512 + 512],
                                      in_=G(f"ytmp{dnj % 2}")[:], accum_op=OP.add))
                        dnj += 1
            mark_here("y0_total", "sY0")
            mark_here("y1_total", "sY1")
            mark_here("yi_total", "sYI")
            wt(g, "y0_total")
            wt(g, "y1_total")
            wt(g, "yi_total")

        if not real:
            sync_p0(None); sync_p3(None); sync_p4(None); sync_p5(None); sync_p6(None)
            scalar_p1(None); scalar_p3(None); scalar_p4(None); scalar_p5(None); scalar_p6(None)
            vector_p1(None); vector_p2(None); vector_p4(None); vector_p6(None)
            tensor_p1(None); tensor_p3(None); tensor_p4(None); tensor_p5(None); tensor_p6(None)
            gpsimd_all(None)
            return

        block = ctx["block"]
        with contextlib.ExitStack() as stA:
            for nm, shape, dt in (
                ("xt0", [128, D], F32), ("xt1", [128, D], F32),
                ("sqs", [128, D], F32),
                ("wva", [128, MC * DBW], F32R), ("wvb", [128, MC * DBW], F32R),
                ("vall", [128, NT * DBW], F32R),
                ("kt0", [128, DBW], F32R), ("kt1", [128, DBW], F32R),
                ("bkvs", [2, 2 * D], F32R),
            ):
                ctx[nm] = stA.enter_context(nc.sbuf_tensor(nm, shape, dt))
            for i in range(6):
                ctx[f"kvp{i}"] = stA.enter_context(nc.psum_tensor(f"kvp{i}", [128, 512], F32))
            ctx["pj0"] = stA.enter_context(nc.psum_tensor("pj0", [128, 512], F32))
            ctx["pj1"] = stA.enter_context(nc.psum_tensor("pj1", [128, 512], F32))

            @block.sync
            def _(s):
                sync_p0(s); sync_p3(s)

            @block.scalar
            def _(s):
                scalar_p1(s); scalar_p3(s)

            @block.vector
            def _(v):
                vector_p1(v); vector_p2(v)

            @block.tensor
            def _(t_):
                tensor_p1(t_); tensor_p3(t_)

        with contextlib.ExitStack() as stBC:
            ctx["ocT"] = stBC.enter_context(nc.sbuf_tensor("ocT", [128, MC * NTOK], F32R))
            with contextlib.ExitStack() as stB:
                for nm, shape, dt in (
                    ("wq0", [128, MC * 128], F32R), ("wq1", [128, MC * 128], F32R),
                    ("qh0", [128, NTOK], F32R),
                    ("ocp", [128, NT * 512], F32),
                ):
                    ctx[nm] = stB.enter_context(nc.sbuf_tensor(nm, shape, dt))
                for nm in ("qp0", "qp1", "at0", "at1", "at2", "tr0", "tr1"):
                    ctx[nm] = stB.enter_context(nc.psum_tensor(nm, [128, 512], F32))

                @block.sync
                def _(s):
                    sync_p4(s)

                @block.scalar
                def _(s):
                    scalar_p4(s)

                @block.vector
                def _(v):
                    vector_p4(v)

                @block.tensor
                def _(t_):
                    tensor_p4(t_)

            with contextlib.ExitStack() as stC:
                for nm, shape, dt in (
                    ("wob0", [128, MC * DBW], F32R),
                    ("bos", [2, D], F32R),
                ):
                    ctx[nm] = stC.enter_context(nc.sbuf_tensor(nm, shape, dt))
                for nm in ("wo0", "wo1"):
                    ctx[nm] = stC.enter_context(nc.psum_tensor(nm, [128, 512], F32))

                @block.sync
                def _(s):
                    sync_p5(s)

                @block.scalar
                def _(s):
                    scalar_p5(s)

                @block.tensor
                def _(t_):
                    tensor_p5(t_)

        with contextlib.ExitStack() as stD:
            for nm, shape, dt in (
                ("hsl", [128, 5 * 1024], F32R),
                ("gtp0", [128, D], F32R), ("gtp1", [128, D], F32R),
                ("utp0", [128, D], F32R), ("utp1", [128, D], F32R),
                ("dts", [128, 5 * D], F32R),
                ("gst", [128, 2048], F32),
            ):
                ctx[nm] = stD.enter_context(nc.sbuf_tensor(nm, shape, dt))
            for i in range(8):
                ctx[f"fp{i}"] = stD.enter_context(nc.psum_tensor(f"fp{i}", [128, 512], F32))

            @block.sync
            def _(s):
                sync_p6(s)

            @block.scalar
            def _(s):
                scalar_p6(s)

            @block.vector
            def _(v):
                vector_p6(v)

            @block.tensor
            def _(t_):
                tensor_p6(t_)

        @block.gpsimd
        def _(g):
            gpsimd_all(g)

    emit(False, {})
    for key in waits_seen:
        assert key in marks, f"wait on unmarked key {key}"

    with contextlib.ExitStack() as st:
        ctx = {}
        for nm in ("sC", "sX0", "sX1", "sGI0", "sGI1", "sWA", "sWB",
                   "sWQ0", "sWQ1", "sWO", "sG0", "sG1", "sU0", "sU1", "sDT",
                   "sKV", "sGW0", "sGW1", "sY0", "sY1", "sYI",
                   "sCG", "sCR", "sCU", "sCD", "sPE", "sVE", "sSE"):
            ctx[nm] = st.enter_context(nc.semaphore(nm))
        for nm, shape, dt in (
            ("xnT", [128, MC * NTOK], F32R),
            ("gi0", [128, 1024], I32), ("gi1", [128, 1024], I32),
            ("gw0", [128, 1024], F32R), ("gw1", [128, 1024], F32R),
            ("gtm", [128, 1024], F32R),
            ("kvb", [128, H * 130], F32), ("kvr", [128, H * 130], F32R),
            ("luts", [128, 16], F32), ("dcols", [128, NT], F32),
            ("bqcs", [128, H], F32), ("idents", [128, 128], F32),
            ("ones", [2, 512], F32R), ("onec", [128, 2], F32R),
            ("stats", [128, 56], F32), ("zs", [128, 4], F32), ("epsc", [128, 1], F32),
            ("ytmp0", [128, 512], F32), ("ytmp1", [128, 512], F32),
        ):
            ctx[nm] = st.enter_context(nc.sbuf_tensor(nm, shape, dt))
        with nc.Block() as block:
            ctx["block"] = block
            emit(True, ctx)
    return nc


def kernel(L0_attn_x, L0_attn_Wq, L0_attn_bq, L0_attn_Wk, L0_attn_bk,
           L0_attn_Wv, L0_attn_bv, L0_attn_Wo, L0_attn_bo,
           L0_ffn_lut, L0_ffn_gate_idx, L0_ffn_up_idx, L0_ffn_down_idx):
    x = np.ascontiguousarray(np.asarray(L0_attn_x, dtype=np.float32)[0])

    def pan_db(w):
        return np.ascontiguousarray(
            np.asarray(w, np.float32).T.reshape(MC, 128, NDB, DBW).transpose(2, 0, 1, 3))

    wq_pan = np.ascontiguousarray(
        np.asarray(L0_attn_Wq, np.float32).T.reshape(MC, 128, H, 128).transpose(2, 0, 1, 3))
    wk_pan = pan_db(L0_attn_Wk)
    wv_pan = pan_db(L0_attn_Wv)
    wo_pan = pan_db(L0_attn_Wo)
    bqc = np.ascontiguousarray(np.asarray(L0_attn_bq, np.float32).reshape(H, 128).T)
    bkv = np.zeros((2, 2 * D), np.float32)
    bkv[0] = np.concatenate([np.asarray(L0_attn_bk, np.float32),
                             np.asarray(L0_attn_bv, np.float32)])
    bo = np.zeros((2, D), np.float32)
    bo[0] = np.asarray(L0_attn_bo, np.float32)
    lut = np.asarray(L0_ffn_lut, np.float32)
    gidxT = np.ascontiguousarray(np.asarray(L0_ffn_gate_idx, np.int32).T)
    uidxT = np.ascontiguousarray(np.asarray(L0_ffn_up_idx, np.int32).T)
    didxT = np.zeros((8 * ICC * 128, D), np.int32)
    didxT[:INTER] = np.asarray(L0_ffn_down_idx, np.int32).T

    def idx_shard(idxT, c):
        out = np.zeros((ICC, 128, D), np.int32)
        for j in range(ICC):
            ic = ICC * c + j
            if ic < ICR:
                blk = idxT[:, ic * 128:(ic + 1) * 128]
                out[j] = blk.reshape(MC, 128, 128).transpose(1, 0, 2).reshape(128, D)
        return out

    decay = np.exp(-(np.arange(8192, dtype=np.float64) / 8192.0)).astype(np.float32)
    ident = np.eye(128, dtype=np.float32)
    _ones = np.ones((128, 512), np.float32)

    nc = build()
    in_maps = []
    for c in range(8):
        dc = np.ascontiguousarray(decay[NTOK * c: NTOK * (c + 1)].reshape(NT, 128).T)
        in_maps.append({
            "x": np.ascontiguousarray(x[NTOK * c: NTOK * (c + 1)]),
            "wq": wq_pan, "wk": wk_pan, "wv": wv_pan, "wo": wo_pan,
            "bqc": bqc, "bkv": bkv, "bo": bo,
            "gidx": idx_shard(gidxT, c), "uidx": idx_shard(uidxT, c),
            "didx": np.ascontiguousarray(
                didxT[768 * c: 768 * (c + 1)].reshape(ICC, 128, D)),
            "lut": lut, "dcol": dc, "ident": ident,
            "onesm": _ones,
        })
    res = run_bass_kernel_spmd(nc, in_maps, core_ids=list(range(8)))
    y = np.concatenate([np.asarray(res.results[c]["y"]) for c in range(8)], axis=0)
    return y.reshape(1, 8192, D).astype(np.float32)


# version 12
# speedup vs baseline: 1.1169x; 1.1169x over previous
import contextlib
import numpy as np
import concourse.bass as bass
import concourse.mybir as mybir
from concourse.bass_utils import run_bass_kernel_spmd

F32 = mybir.dt.float32
F32R = mybir.dt.float32r
I32 = mybir.dt.int32
AF = mybir.ActivationFunctionType
OP = mybir.AluOpType

D = 2048
NTOK = 1024
NT = 8
MC = 16
H = 16
NDB = 8
DBW = 256
INTER = 5504
ICR = 43
ICC = 6
EPS_LN = 1e-5
EPS_AT = 1e-6
SLAB_SIZES = [4, 4, 4, 4, 4, 4, 4, 4, 4, 4, 3]
SLABS = []
_i0 = 0
for _s in SLAB_SIZES:
    SLABS.append(list(range(_i0, _i0 + _s)))
    _i0 += _s
INV_D = 1.0 / D

KV_BANK = [hh // 3 for hh in range(H)]
KV_OFF = [(hh % 3) * 130 for hh in range(H)]
_first_h = {}
_last_h = {}
for _hh in range(H):
    _b = KV_BANK[_hh]
    _first_h.setdefault(_b, _hh)
    _last_h[_b] = _hh

GTILES = [(m, j, hf) for m in range(3) for j in range(ICC) for hf in range(2)]
NGT = len(GTILES)
WEAVE_AFTER_HEAD = {2 + i: 24 + i for i in range(12)}


def build():
    nc = bass.Bass("TRN2", target_bir_lowering=False, debug=False,
                   detect_race_conditions=False)

    x_h = nc.dram_tensor("x", [NTOK, D], F32, kind="ExternalInput").ap()
    wq_h = nc.dram_tensor("wq", [H, MC, 128, 128], F32R, kind="ExternalInput").ap()
    wk_h = nc.dram_tensor("wk", [NDB, MC, 128, DBW], F32R, kind="ExternalInput").ap()
    wv_h = nc.dram_tensor("wv", [NDB, MC, 128, DBW], F32R, kind="ExternalInput").ap()
    wo_h = nc.dram_tensor("wo", [NDB, MC, 128, DBW], F32R, kind="ExternalInput").ap()
    bqc_h = nc.dram_tensor("bqc", [128, H], F32, kind="ExternalInput").ap()
    bkv_h = nc.dram_tensor("bkv", [2, 2 * D], F32R, kind="ExternalInput").ap()
    bo_h = nc.dram_tensor("bo", [2, D], F32R, kind="ExternalInput").ap()
    gidx_h = nc.dram_tensor("gidx", [ICC, 128, D], I32, kind="ExternalInput").ap()
    uidx_h = nc.dram_tensor("uidx", [ICC, 128, D], I32, kind="ExternalInput").ap()
    didx_h = nc.dram_tensor("didx", [ICC, 128, D], I32, kind="ExternalInput").ap()
    lut_h = nc.dram_tensor("lut", [16], F32, kind="ExternalInput").ap()
    dcol_h = nc.dram_tensor("dcol", [128, NT], F32, kind="ExternalInput").ap()
    id_h = nc.dram_tensor("ident", [128, 128], F32, kind="ExternalInput").ap()
    ones_h = nc.dram_tensor("onesm", [128, 512], F32R, kind="ExternalInput").ap()
    y_h = nc.dram_tensor("y", [NTOK, D], F32, kind="ExternalOutput").ap()

    sh_h = [nc.dram_tensor(n, [ICC, 128, D], F32R) for n in ("sh_g", "sh_u", "sh_d")]
    bl_h = [nc.dram_tensor(n, [8 * ICC, 128, D], F32R, addr_space="Shared")
            for n in ("bl_g", "bl_u", "bl_d")]
    kvin_h = nc.dram_tensor("kv_in", [128, H * 130], F32)
    kvout_h = nc.dram_tensor("kv_out", [128, H * 130], F32, addr_space="Shared")

    marks = {}
    state = {"real": False}
    waits_seen = []

    class Ctr:
        def __init__(self):
            self.v = 0

        def inc(self, n=1):
            self.v += n
            return self.v

    def setmark(key, v):
        if not state["real"]:
            assert key not in marks, f"dup mark {key}"
            marks[key] = v
        else:
            assert marks[key] == v, (key, marks[key], v)

    def emit(real, ctx):
        state["real"] = real
        from collections import defaultdict
        ctrs = defaultdict(Ctr)

        def G(name):
            return ctx.get(name)

        def I(sem, n, key, call):
            if real:
                inst = call()
                if n:
                    inst.then_inc(ctx[sem], n)
            if n:
                v = ctrs[sem].inc(n)
                if key is not None:
                    setmark(key, (sem, v))

        def wt(eng, key):
            if not real:
                waits_seen.append(key)
                return
            nm, v = marks[key]
            eng.wait_ge(ctx[nm], v)

        def mark_here(key, sem):
            setmark(key, (sem, ctrs[sem].v))

        def vw(v):
            if real:
                v.wait_ge(ctx["sVE"], ctrs["sVE"].v)

        def sw(s):
            if real:
                s.wait_ge(ctx["sSE"], ctrs["sSE"].v)

        def sync_p0(s):
            I("sC", 16, "ld_ident", lambda: s.dma_start(out=G("idents")[:], in_=id_h))
            I("sC", 16, "ld_lut", lambda: s.dma_start(
                out=G("luts")[:], in_=lut_h[None, :].broadcast_to([128, 16])))
            I("sC", 16, "ld_dcol", lambda: s.dma_start(out=G("dcols")[:], in_=dcol_h))
            I("sC", 16, "ld_bqc", lambda: s.dma_start(out=G("bqcs")[:], in_=bqc_h))
            I("sC", 16, "ld_bkv", lambda: s.dma_start(out=G("bkvs")[:], in_=bkv_h))
            for t in range(2):
                I(f"sX{t % 2}", 16, f"ld_x{t}", lambda t=t: s.dma_start(
                    out=G(f"xt{t}")[:], in_=x_h[t * 128:(t + 1) * 128, :]))
            I("sC", 16, "ld_ones", lambda: s.dma_start(out=G("ones")[:], in_=ones_h[0:2, :]))
            I("sC", 16, "ld_onec", lambda: s.dma_start(out=G("onec")[:], in_=ones_h[:, 0:2], allow_slow_non_contiguous=True))
            I("sWA", 16, "ld_wv0", lambda: s.dma_start(
                out=G("wva")[:].rearrange("p (k f) -> p k f", k=MC), in_=wv_h[0].rearrange("k p f -> p k f")))
            for t in range(2, NT):
                wt(s, f"pe_tp_end{t - 2}")
                wt(s, f"gp_yinit{t - 2}")
                I(f"sX{t % 2}", 16, f"ld_x{t}", lambda t=t: s.dma_start(
                    out=G(f"xt{t % 2}")[:], in_=x_h[t * 128:(t + 1) * 128, :]))

        def sync_p3(s):
            for db in range(NDB):
                if db >= 1:
                    wt(s, f"pe_k_end{db - 1}")
                I("sWB", 16, f"ld_wk{db}", lambda db=db: s.dma_start(
                    out=G("wvb")[:].rearrange("p (k f) -> p k f", k=MC), in_=wk_h[db].rearrange("k p f -> p k f")))
                if db < NDB - 1:
                    wt(s, f"pe_v_end{db}")
                    I("sWA", 16, f"ld_wv{db + 1}", lambda db=db: s.dma_start(
                        out=G("wva")[:].rearrange("p (k f) -> p k f", k=MC), in_=wv_h[db + 1].rearrange("k p f -> p k f")))

        def sync_p4(s):
            for h in range(H):
                if h < 2:
                    wt(s, "pe_p3_end")
                else:
                    wt(s, f"pe_q_end{h - 2}")
                I(f"sWQ{h % 2}", 16, f"ld_wq{h}", lambda h=h: s.dma_start(
                    out=G(f"wq{h % 2}")[:].rearrange("p (k f) -> p k f", k=MC), in_=wq_h[h].rearrange("k p f -> p k f")))

        def sync_p5(s):
            wt(s, "pe_attn_end")
            I("sWO", 16, "ld_bo", lambda: s.dma_start(out=G("bos")[:], in_=bo_h))
            for ob in range(NDB):
                if ob < 1:
                    wt(s, "pe_attn_end")
                else:
                    wt(s, f"pe_wo_end{ob - 1}")
                I("sWO", 16, f"ld_wo{ob}", lambda ob=ob: s.dma_start(
                    out=G("wob0")[:].rearrange("p (k f) -> p k f", k=MC), in_=wo_h[ob].rearrange("k p f -> p k f")))

        def sync_p6(s):
            first = {"g": True, "u": True}
            for si, slab in enumerate(SLABS):
                for ic in slab:
                    for mat, mi, cck in (("g", 0, "cc_ag_g"), ("u", 1, "cc_ag_u")):
                        if first[mat]:
                            wt(s, cck)
                            wt(s, "pe_wo_end7")
                            first[mat] = False
                        if ic >= 2:
                            wt(s, f"pe_{mat}end{ic - 2}")
                        I(f"s{mat.upper()}{ic % 2}", 16, f"ld_{mat}{ic}", lambda mat=mat, mi=mi, ic=ic: s.dma_start(
                            out=G(f"{mat}tp{ic % 2}")[:], in_=bl_h[mi].ap()[ic]))
                for jj, ic in enumerate(slab):
                    if si == 0 and jj == 0:
                        wt(s, "cc_ag_d")
                        wt(s, "pe_wo_end7")
                    if si >= 1 and jj == 0:
                        wt(s, f"pe_dn_end{si - 1}")
                    I("sDT", 16, f"ld_dt{si}_{jj}", lambda si=si, jj=jj, ic=ic: s.dma_start(
                        out=G("dts")[:, jj * D:(jj + 1) * D], in_=bl_h[2].ap()[ic]))

        def scalar_p1(s):
            wt(s, "ld_onec")
            def stat_pair(t):
                wt(s, f"ld_x{t}")
                I("sSE", 1, f"se_ssq{t}", lambda t=t: s.activation(
                    G("sqs")[:], G(f"xt{t % 2}")[:], AF.Square,
                    accum_out=G("stats")[:, 8 + t:9 + t]))
                wt(s, f"ve_var{t}")
                if t == 0:
                    wt(s, "ve_eps")
                I("sSE", 1, f"se_sd{t}", lambda t=t: s.activation(
                    G("stats")[:, 40 + t:41 + t], G("stats")[:, 41 + t:42 + t],
                    AF.Sqrt, bias=G("epsc")[:, 0:1]))
            stat_pair(0)
            stat_pair(1)
            for t in range(NT):
                for mc_ in range(MC):
                    wt(s, f"pe_tp{t}_{mc_}")
                    I("sSE", 1, f"se_ev{t}_{mc_}", lambda t=t, mc_=mc_: s.copy(
                        G("xnT")[:, mc_ * NTOK + t * 128: mc_ * NTOK + (t + 1) * 128],
                        G(f"pj{(t * MC + mc_) % 2}")[:, 0:128]))
                if t + 2 < NT:
                    stat_pair(t + 2)

        def scalar_p3(s):
            for db in range(NDB):
                for t in range(NT):
                    wt(s, f"pe_v{db}_{t}")
                    if db >= 1:
                        wt(s, f"pe_kv{db - 1}_{t}")
                    I("sSE", 1, f"se_vev{db}_{t}", lambda db=db, t=t: s.copy(
                        G("vall")[:, t * DBW:(t + 1) * DBW],
                        G(f"pj{t % 2}")[:, 0:DBW]))
                for t in range(NT):
                    wt(s, f"pe_kp{db}_{t}")
                    if t >= 2:
                        wt(s, f"pe_kv{db}_{t - 2}")
                    elif db >= 1:
                        wt(s, f"pe_kv{db - 1}_{t + 6}")
                    I("sSE", 1, f"se_kev{db}_{t}", lambda db=db, t=t: s.activation(
                        G(f"kt{t % 2}")[:], G(f"pj{t % 2}")[:, 0:DBW],
                        AF.Relu, scale=G("dcols")[:, t:t + 1]))
            wt(s, "pe_p3_end")
            for b in range(6):
                size = 390 if b < 5 else 130
                I("sSE", 1, f"se_kvev{b}", lambda b=b, size=size: s.copy(
                    G("kvb")[:, b * 390: b * 390 + size],
                    G(f"kvp{b}")[:, 0:size]))
            sw(s)
            I("sKV", 16, "se_kvin", lambda: s.dma_start(out=kvin_h.ap(), in_=G("kvb")[:]))

        def scalar_p4(s):
            for h in range(H):
                for nb in range(2):
                    wt(s, f"pe_q{h}_{nb}")
                    if h >= 1:
                        wt(s, f"pe_at{h - 1}_7")
                    I("sSE", 1, f"se_qev{h}_{nb}", lambda h=h, nb=nb: s.activation(
                        G("qh0")[:, nb * 512:(nb + 1) * 512],
                        G(f"qp{nb}")[:], AF.Relu, bias=G("bqcs")[:, h:h + 1]))
                for t in range(NT):
                    i = h * NT + t
                    wt(s, f"ve_z{h}_{t}")
                    if h >= 2:
                        wt(s, f"pe_troc{h // 2 - 1}_{t * 2 + h % 2}")
                    I("sSE", 1, f"se_ocs{h}_{t}", lambda h=h, t=t, i=i: s.activation(
                        G("ocp")[:, t * 256 + (h % 2) * 128: t * 256 + (h % 2) * 128 + 128],
                        G(f"at{i % 3}")[:, 0:128],
                        AF.Copy, scale=G("zs")[:, 2 * (i % 2) + 1: 2 * (i % 2) + 2]))
                if h % 2 == 1:
                    g = h // 2
                    for j in range(16):
                        tt, hc = j // 2, j % 2
                        wt(s, f"pe_troc{g}_{j}")
                        I("sSE", 1, f"se_ocT{g}_{j}", lambda g=g, j=j, tt=tt, hc=hc: s.copy(
                            G("ocT")[:, (2 * g + hc) * NTOK + tt * 128:
                                     (2 * g + hc) * NTOK + (tt + 1) * 128],
                            G(f"tr{j % 2}")[:, 0:128]))

        def scalar_p5(s):
            for ob in range(NDB):
                for t in range(NT):
                    i = ob * NT + t
                    wt(s, f"pe_wo{ob}_{t}")
                    if i >= 2:
                        wt(s, f"gp_ywo{i - 2}")
                    I("sSE", 1, f"se_woev{ob}_{t}", lambda ob=ob, t=t, i=i: s.copy(
                        G(f"ytw{i % 2}")[:, 0:DBW], G(f"wo{i % 2}")[:, 0:DBW]))

        def scalar_p6(s):
            dnj = 0
            for si, slab in enumerate(SLABS):
                for ic in slab:
                    for nb in range(2):
                        wt(s, f"pe_gate{ic}_{nb}")
                        if ic >= 2:
                            wt(s, f"ve_h{ic - 2}")
                        I("sSE", 1, f"se_silu{ic}_{nb}", lambda ic=ic, nb=nb: s.activation(
                            G("gst")[:, (ic % 2) * 1024 + nb * 512:
                                     (ic % 2) * 1024 + (nb + 1) * 512],
                            G(f"fp{4 * (ic % 2) + nb}")[:], AF.Silu))
                for ot in range(NT):
                    wt(s, f"pe_dn{si}_{ot}")
                    if dnj < 2:
                        wt(s, f"gp_ywo{62 + dnj}")
                    else:
                        wt(s, f"gp_ydn{dnj - 2}")
                    for ob in range(4):
                        I("sSE", 1,
                          f"se_dnev4{si}_{ot}" if ob == 3 else None,
                          lambda si=si, ot=ot, ob=ob, dnj=dnj: s.copy(
                              G(f"ytw{dnj % 2}")[:, ob * 512:(ob + 1) * 512],
                              G(f"fp{4 * (ot % 2) + ob}")[:]))
                    dnj += 1

        def gather_tile(v, k):
            wt(v, f"ld_gi{k}")
            if k >= 2:
                wt(v, f"gp_gw{k - 2}")
            I("sVE", 0, None, lambda k=k: v.tensor_scalar(
                G(f"gw{k % 2}")[:], G(f"gi{k % 2}")[:], 0.0, G("luts")[:, 0:1],
                OP.is_equal, OP.mult))
            for val in range(1, 16):
                I("sVE", 0, None, lambda k=k, val=val: v.tensor_scalar(
                    G("gtm")[:], G(f"gi{k % 2}")[:], float(val),
                    G("luts")[:, val:val + 1], OP.is_equal, OP.mult))
                last = val == 15
                I("sVE", 1 if last else 0, f"ve_g{k}" if last else None,
                  lambda k=k: v.tensor_add(G(f"gw{k % 2}")[:], G(f"gw{k % 2}")[:],
                                           G("gtm")[:]))

        def vector_p1(v):
            wt(v, "ld_onec")
            I("sVE", 1, "ve_eps", lambda: v.memset(G("epsc")[:], EPS_LN))
            for t in range(NT):
                wt(v, f"ld_x{t}")
                I("sVE", 1, None, lambda t=t: v.reduce_sum(
                    G("stats")[:, t:t + 1], G(f"xt{t % 2}")[:], mybir.AxisListType.X))
                wt(v, f"se_ssq{t}")
                vw(v)
                I("sVE", 1, None, lambda t=t: v.tensor_single_scalar(
                    G("stats")[:, 16 + t:17 + t], G("stats")[:, t:t + 1], INV_D, OP.mult))
                I("sVE", 1, None, lambda t=t: v.tensor_single_scalar(
                    G("stats")[:, 24 + t:25 + t], G("stats")[:, 8 + t:9 + t], INV_D, OP.mult))
                vw(v)
                I("sVE", 1, None, lambda t=t: v.tensor_tensor(
                    G("stats")[:, 33 + t:34 + t], G("stats")[:, 16 + t:17 + t],
                    G("stats")[:, 16 + t:17 + t], OP.mult))
                vw(v)
                I("sVE", 1, f"ve_var{t}", lambda t=t: v.tensor_tensor(
                    G("stats")[:, 41 + t:42 + t], G("stats")[:, 24 + t:25 + t],
                    G("stats")[:, 33 + t:34 + t], OP.subtract))
                wt(v, f"se_sd{t}")
                wt(v, f"gp_yinit{t}")
                I("sVE", 1, None, lambda t=t: v.reciprocal(
                    G("stats")[:, 48 + t:49 + t], G("stats")[:, 40 + t:41 + t]))
                vw(v)
                I("sVE", 1, f"ve_ln{t}", lambda t=t: v.tensor_scalar(
                    G(f"xt{t % 2}")[:], G(f"xt{t % 2}")[:],
                    G("stats")[:, 16 + t:17 + t], G("stats")[:, 48 + t:49 + t],
                    OP.subtract, OP.mult))

        def vector_p2(v):
            for k in range(24):
                gather_tile(v, k)

        def vector_p4(v):
            for h in range(H):
                for t in range(NT):
                    i = h * NT + t
                    zi = 2 * (i % 2)
                    wt(v, f"pe_at{h}_{t}")
                    if i >= 2:
                        wt(v, f"se_ocs{(i - 2) // NT}_{(i - 2) % NT}")
                    I("sVE", 1, None, lambda i=i, t=t, zi=zi: v.tensor_tensor(
                        G("zs")[:, zi:zi + 1], G(f"at{i % 3}")[:, 128:129],
                        G("dcols")[:, t:t + 1], OP.mult))
                    vw(v)
                    I("sVE", 1, None, lambda zi=zi: v.tensor_single_scalar(
                        G("zs")[:, zi + 1:zi + 2], G("zs")[:, zi:zi + 1], EPS_AT, OP.add))
                    vw(v)
                    I("sVE", 1, None, lambda zi=zi: v.reciprocal(
                        G("zs")[:, zi:zi + 1], G("zs")[:, zi + 1:zi + 2]))
                    vw(v)
                    I("sVE", 1, f"ve_z{h}_{t}", lambda t=t, zi=zi: v.tensor_tensor(
                        G("zs")[:, zi + 1:zi + 2], G("zs")[:, zi:zi + 1],
                        G("dcols")[:, t:t + 1], OP.mult))


        def vector_p5d(v):
            for k in range(24, 36):
                gather_tile(v, k)

        def vector_p6(v):
            for si, slab in enumerate(SLABS):
                for jj, ic in enumerate(slab):
                    if jj == 0 and si >= 1:
                        wt(v, f"pe_dn_end{si - 1}")
                    for nb in range(2):
                        wt(v, f"pe_up{ic}_{nb}")
                        wt(v, f"se_silu{ic}_{nb}")
                        last = nb == 1
                        I("sVE", 1 if last else 0, f"ve_h{ic}" if last else None,
                          lambda ic=ic, jj=jj, nb=nb: v.tensor_tensor(
                              G("hsl")[:, jj * 1024 + nb * 512: jj * 1024 + (nb + 1) * 512],
                              G("gst")[:, (ic % 2) * 1024 + nb * 512:
                                       (ic % 2) * 1024 + (nb + 1) * 512],
                              G(f"fp{4 * (ic % 2) + 2 + nb}")[:], OP.mult))

        def tensor_p1(t_):
            wt(t_, "ld_onec")
            for t in range(NT):
                wt(t_, f"ve_ln{t}")
                for mc_ in range(MC):
                    slot = t * MC + mc_
                    if slot >= 2:
                        wt(t_, f"se_ev{(slot - 2) // MC}_{(slot - 2) % MC}")
                    I("sPE", 1, f"pe_tp{t}_{mc_}", lambda t=t, mc_=mc_, slot=slot: t_.transpose(
                        G(f"pj{slot % 2}")[:, 0:128],
                        G(f"xt{t % 2}")[:, mc_ * 128:(mc_ + 1) * 128],
                        G("idents")[:]))
                mark_here(f"pe_tp_end{t}", "sPE")

        def proj_tile_tm(t_, pjname, xt_t, rhsname, bias_off, key):
            for mc_ in range(MC):
                I("sPE", 0, None, lambda mc_=mc_: t_.matmul(
                    G(pjname)[:, 0:DBW],
                    G("xnT")[:, mc_ * NTOK + xt_t * 128: mc_ * NTOK + (xt_t + 1) * 128],
                    G(rhsname)[:, mc_ * DBW:(mc_ + 1) * DBW],
                    start=(mc_ == 0), stop=False))
            I("sPE", 1, key, lambda: t_.matmul(
                G(pjname)[:, 0:DBW], G("ones")[0:2, 0:128],
                G("bkvs")[0:2, bias_off:bias_off + DBW],
                start=False, stop=True))

        def tensor_p3(t_):
            wt(t_, "ld_onec")
            wt(t_, f"se_ev{NT - 1}_{MC - 1}")
            kv_started = set()
            kv_count = {}
            total_per_bank = {}
            for hh in range(H):
                total_per_bank[KV_BANK[hh]] = total_per_bank.get(KV_BANK[hh], 0) + NT * 2
            for db in range(NDB):
                wt(t_, f"ld_wv{db}")
                for t in range(NT):
                    if t >= 2:
                        wt(t_, f"se_vev{db}_{t - 2}")
                    elif db >= 1:
                        wt(t_, f"se_kev{db - 1}_{6 + t}")
                    proj_tile_tm(t_, f"pj{t % 2}", t, "wva", D + db * DBW, f"pe_v{db}_{t}")
                mark_here(f"pe_v_end{db}", "sPE")
                wt(t_, f"ld_wk{db}")
                for t in range(NT):
                    if t >= 2:
                        wt(t_, f"se_kev{db}_{t - 2}")
                    else:
                        wt(t_, f"se_vev{db}_{6 + t}")
                    proj_tile_tm(t_, f"pj{t % 2}", t, "wvb", db * DBW, f"pe_kp{db}_{t}")
                    wt(t_, f"se_kev{db}_{t}")
                    for hl in range(2):
                        hh = db * 2 + hl
                        b = KV_BANK[hh]
                        off = KV_OFF[hh]
                        start = b not in kv_started
                        kv_started.add(b)
                        kv_count[b] = kv_count.get(b, 0) + 2
                        stop = kv_count[b] == total_per_bank[b]
                        I("sPE", 0, None, lambda b=b, off=off, t=t, hl=hl, start=start:
                          t_.matmul(G(f"kvp{b}")[:, off:off + 128],
                                    G(f"kt{t % 2}")[:, hl * 128:(hl + 1) * 128],
                                    G("vall")[:, t * DBW + hl * 128: t * DBW + (hl + 1) * 128],
                                    start=start, stop=False))
                        I("sPE", 1, None, lambda b=b, off=off, t=t, hl=hl, stop=stop:
                          t_.matmul(G(f"kvp{b}")[:, off + 128:off + 130],
                                    G(f"kt{t % 2}")[:, hl * 128:(hl + 1) * 128],
                                    G("onec")[:, 0:2],
                                    start=False, stop=stop))
                    mark_here(f"pe_kv{db}_{t}", "sPE")
                mark_here(f"pe_k_end{db}", "sPE")
            mark_here("pe_p3_end", "sPE")

        def tensor_p4(t_):
            for b in range(6):
                wt(t_, f"se_kvev{b}")
            for h in range(H):
                wt(t_, f"ld_wq{h}")
                for nb in range(2):
                    if h >= 1:
                        wt(t_, f"se_qev{h - 1}_{nb}")
                    for mc_ in range(MC):
                        last = mc_ == MC - 1
                        I("sPE", 1 if last else 0, f"pe_q{h}_{nb}" if last else None,
                          lambda h=h, nb=nb, mc_=mc_, last=last: t_.matmul(
                              G(f"qp{nb}")[:],
                              G(f"wq{h % 2}")[:, mc_ * 128:(mc_ + 1) * 128],
                              G("xnT")[:, mc_ * NTOK + nb * 512: mc_ * NTOK + nb * 512 + 512],
                              start=(mc_ == 0), stop=last))
                mark_here(f"pe_q_end{h}", "sPE")
                wt(t_, f"se_qev{h}_0")
                wt(t_, f"se_qev{h}_1")
                if h == 0:
                    wt(t_, "gp_kvr")
                for t in range(NT):
                    i = h * NT + t
                    if i >= 3:
                        wt(t_, f"ve_z{(i - 3) // NT}_{(i - 3) % NT}")
                        wt(t_, f"se_ocs{(i - 3) // NT}_{(i - 3) % NT}")
                    I("sPE", 1, f"pe_at{h}_{t}", lambda h=h, t=t, i=i: t_.matmul(
                        G(f"at{i % 3}")[:, 0:130],
                        G("qh0")[:, t * 128:(t + 1) * 128],
                        G("kvr")[:, h * 130:(h + 1) * 130],
                        start=True, stop=True))
                if h % 2 == 1:
                    g = h // 2
                    for j in range(16):
                        tt, hc = j // 2, j % 2
                        wt(t_, f"se_ocs{2 * g + hc}_{tt}")
                        if j >= 2:
                            wt(t_, f"se_ocT{g}_{j - 2}")
                        elif g >= 1:
                            wt(t_, f"se_ocT{g - 1}_{14 + j}")
                        I("sPE", 1, f"pe_troc{g}_{j}", lambda g=g, j=j, tt=tt, hc=hc: t_.transpose(
                            G(f"tr{j % 2}")[:, 0:128],
                            G("ocp")[:, tt * 256 + hc * 128: tt * 256 + (hc + 1) * 128],
                            G("idents")[:]))
            mark_here("pe_attn_end", "sPE")

        def tensor_p5(t_):
            wt(t_, "se_ocT7_15")
            wt(t_, "ve_z15_7")
            for ob in range(NDB):
                wt(t_, f"ld_wo{ob}")
                for t in range(NT):
                    i = ob * NT + t
                    if i >= 2:
                        wt(t_, f"se_woev{(i - 2) // NT}_{(i - 2) % NT}")
                    for mc_ in range(MC):
                        I("sPE", 0, None, lambda t=t, mc_=mc_, i=i, ob=ob: t_.matmul(
                            G(f"wo{i % 2}")[:, 0:DBW],
                            G("ocT")[:, mc_ * NTOK + t * 128: mc_ * NTOK + (t + 1) * 128],
                            G("wob0")[:, mc_ * DBW:(mc_ + 1) * DBW],
                            start=(mc_ == 0), stop=False))
                    I("sPE", 1, f"pe_wo{ob}_{t}", lambda ob=ob, i=i: t_.matmul(
                        G(f"wo{i % 2}")[:, 0:DBW], G("ones")[0:2, 0:128],
                        G("bos")[0:2, ob * DBW:(ob + 1) * DBW],
                        start=False, stop=True))
                mark_here(f"pe_wo_end{ob}", "sPE")

        def tensor_p6(t_):
            for si, slab in enumerate(SLABS):
                for jj, ic in enumerate(slab):
                    for mat in ("g", "u"):
                        wt(t_, f"ld_{mat}{ic}")
                        if si == 0 and jj == 0 and mat == "g":
                            wt(t_, "se_woev7_7")
                        if si >= 1 and jj == 0 and mat == "g":
                            for ot in range(NT):
                                wt(t_, f"se_dnev4{si - 1}_{ot}")
                        if ic >= 2:
                            if mat == "g":
                                wt(t_, f"se_silu{ic - 2}_0")
                                wt(t_, f"se_silu{ic - 2}_1")
                            else:
                                wt(t_, f"ve_h{ic - 2}")
                        for nb in range(2):
                            bank = (4 * (ic % 2)) + (0 if mat == "g" else 2) + nb
                            mmkey = (f"pe_gate{ic}_{nb}" if mat == "g"
                                     else f"pe_up{ic}_{nb}")
                            for mc_ in range(MC):
                                last = mc_ == MC - 1
                                I("sPE", 1 if last else 0, mmkey if last else None,
                                  lambda mat=mat, ic=ic, nb=nb, mc_=mc_, bank=bank, last=last:
                                  t_.matmul(G(f"fp{bank}")[:],
                                            G(f"{mat}tp{ic % 2}")[:, mc_ * 128:(mc_ + 1) * 128],
                                            G("xnT")[:, mc_ * NTOK + nb * 512:
                                                     mc_ * NTOK + nb * 512 + 512],
                                            start=(mc_ == 0), stop=last))
                        mark_here(f"pe_{mat}end{ic}", "sPE")
                for ic2 in slab:
                    wt(t_, f"ve_h{ic2}")
                wt(t_, f"ld_dt{si}_{len(slab) - 1}")
                wt(t_, f"se_silu{slab[-1]}_0")
                wt(t_, f"se_silu{slab[-1]}_1")
                for ot in range(NT):
                    if ot >= 2:
                        wt(t_, f"se_dnev4{si}_{ot - 2}")
                    for jj, ic in enumerate(slab):
                        for ob in range(4):
                            last = jj == len(slab) - 1 and ob == 3
                            I("sPE", 1 if last else 0,
                              f"pe_dn{si}_{ot}" if last else None,
                              lambda jj=jj, ob=ob, ot=ot, last=last: t_.matmul(
                                  G(f"fp{4 * (ot % 2) + ob}")[:],
                                  G("hsl")[:, jj * 1024 + ot * 128: jj * 1024 + (ot + 1) * 128],
                                  G("dts")[:, jj * D + ob * 512: jj * D + ob * 512 + 512],
                                  start=(jj == 0), stop=(jj == len(slab) - 1)))
                mark_here(f"pe_dn_end{si}", "sPE")

        def gpsimd_all(g):
            for k in range(2):
                m, j, hf = GTILES[k]
                I(f"sGI{k % 2}", 16, f"ld_gi{k}", lambda k=k, m=m, j=j, hf=hf: g.dma_start(
                    out=G(f"gi{k % 2}")[:],
                    in_=[gidx_h, uidx_h, didx_h][m][j, :, hf * 1024:(hf + 1) * 1024]))
            for t in range(NT):
                wt(g, f"ld_x{t}")
                I("sYI", 16, f"gp_yinit{t}", lambda t=t: g.dma_start(
                    out=y_h[t * 128:(t + 1) * 128, :], in_=G(f"xt{t % 2}")[:]))
            for k in range(NGT):
                m, j, hf = GTILES[k]
                wt(g, f"ve_g{k}")
                I(f"sGW{k % 2}", 16, f"gp_gw{k}", lambda k=k, m=m, j=j, hf=hf: g.dma_start(
                    out=sh_h[m].ap()[j, :, hf * 1024:(hf + 1) * 1024],
                    in_=G(f"gw{k % 2}")[:]))
                if k + 2 < NGT:
                    k2 = k + 2
                    m2, j2, hf2 = GTILES[k2]
                    I(f"sGI{k2 % 2}", 16, f"ld_gi{k2}",
                      lambda k2=k2, m2=m2, j2=j2, hf2=hf2: g.dma_start(
                          out=G(f"gi{k2 % 2}")[:],
                          in_=[gidx_h, uidx_h, didx_h][m2][j2, :, hf2 * 1024:(hf2 + 1) * 1024]))
                if k == 11:
                    wt(g, "gp_gw10")
                    wt(g, "gp_gw11")
                    I("sCG", 1, "cc_ag_g", lambda: g.collective_compute(
                        "AllGather", OP.bypass, replica_groups=[list(range(8))],
                        ins=[sh_h[0].ap().opt()], outs=[bl_h[0].ap().opt()]))
                if k == 23:
                    wt(g, "gp_gw22")
                    wt(g, "gp_gw23")
                    wt(g, "se_kvin")
                    I("sCR", 1, "cc_ar", lambda: g.collective_compute(
                        "AllReduce", OP.add, replica_groups=[list(range(8))],
                        ins=[kvin_h.ap().opt()], outs=[kvout_h.ap().opt()]))
                    I("sCU", 1, "cc_ag_u", lambda: g.collective_compute(
                        "AllGather", OP.bypass, replica_groups=[list(range(8))],
                        ins=[sh_h[1].ap().opt()], outs=[bl_h[1].ap().opt()]))
                    wt(g, "cc_ar")
                    I("sKR", 16, "gp_kvr", lambda: g.dma_start(
                        out=G("kvr")[:], in_=kvout_h.ap()))
                if k == 35:
                    wt(g, "gp_gw34")
                    wt(g, "gp_gw35")
                    I("sCD", 1, "cc_ag_d", lambda: g.collective_compute(
                        "AllGather", OP.bypass, replica_groups=[list(range(8))],
                        ins=[sh_h[2].ap().opt()], outs=[bl_h[2].ap().opt()]))
            for ob in range(NDB):
                for t in range(NT):
                    i = ob * NT + t
                    wt(g, f"se_woev{ob}_{t}")
                    I(f"sY{i % 2}", 16, f"gp_ywo{i}", lambda ob=ob, t=t, i=i: g.dma_start(
                        out=y_h[t * 128:(t + 1) * 128, ob * DBW:(ob + 1) * DBW],
                        in_=G(f"ytw{i % 2}")[:, 0:DBW], accum_op=OP.add))
            dnj = 0
            for si in range(len(SLABS)):
                for ot in range(NT):
                    wt(g, f"se_dnev4{si}_{ot}")
                    I(f"sY{dnj % 2}", 16, f"gp_ydn{dnj}", lambda ot=ot, dnj=dnj:
                      g.dma_start(out=y_h[ot * 128:(ot + 1) * 128, :],
                                  in_=G(f"ytw{dnj % 2}")[:], accum_op=OP.add))
                    dnj += 1
            mark_here("y0_total", "sY0")
            mark_here("y1_total", "sY1")
            mark_here("yi_total", "sYI")
            wt(g, "y0_total")
            wt(g, "y1_total")
            wt(g, "yi_total")

        if not real:
            sync_p0(None); sync_p3(None); sync_p4(None); sync_p5(None); sync_p6(None)
            scalar_p1(None); scalar_p3(None); scalar_p4(None); scalar_p5(None); scalar_p6(None)
            vector_p1(None); vector_p2(None); vector_p4(None); vector_p5d(None); vector_p6(None)
            tensor_p1(None); tensor_p3(None); tensor_p4(None); tensor_p5(None); tensor_p6(None)
            gpsimd_all(None)
            return

        block = ctx["block"]
        with contextlib.ExitStack() as stA:
            for nm, shape, dt in (
                ("xt0", [128, D], F32), ("xt1", [128, D], F32),
                ("sqs", [128, D], F32),
                ("wva", [128, MC * DBW], F32R), ("wvb", [128, MC * DBW], F32R),
                ("vall", [128, NT * DBW], F32R),
                ("kt0", [128, DBW], F32R), ("kt1", [128, DBW], F32R),
                ("bkvs", [2, 2 * D], F32R), ("kvb", [128, H * 130], F32),
            ):
                ctx[nm] = stA.enter_context(nc.sbuf_tensor(nm, shape, dt))
            for i in range(6):
                ctx[f"kvp{i}"] = stA.enter_context(nc.psum_tensor(f"kvp{i}", [128, 512], F32))
            ctx["pj0"] = stA.enter_context(nc.psum_tensor("pj0", [128, 512], F32))
            ctx["pj1"] = stA.enter_context(nc.psum_tensor("pj1", [128, 512], F32))

            @block.sync
            def _(s):
                sync_p0(s); sync_p3(s)

            @block.scalar
            def _(s):
                scalar_p1(s); scalar_p3(s)

            @block.vector
            def _(v):
                vector_p1(v); vector_p2(v)

            @block.tensor
            def _(t_):
                tensor_p1(t_); tensor_p3(t_)

        with contextlib.ExitStack() as stBC:
            ctx["ocT"] = stBC.enter_context(nc.sbuf_tensor("ocT", [128, MC * NTOK], F32R))
            with contextlib.ExitStack() as stB:
                for nm, shape, dt in (
                    ("wq0", [128, MC * 128], F32R), ("wq1", [128, MC * 128], F32R),
                    ("qh0", [128, NTOK], F32R),
                    ("ocp", [128, NT * 256], F32),
                ):
                    ctx[nm] = stB.enter_context(nc.sbuf_tensor(nm, shape, dt))
                for nm in ("qp0", "qp1", "at0", "at1", "at2", "tr0", "tr1"):
                    ctx[nm] = stB.enter_context(nc.psum_tensor(nm, [128, 512], F32))

                @block.sync
                def _(s):
                    sync_p4(s)

                @block.scalar
                def _(s):
                    scalar_p4(s)

                @block.vector
                def _(v):
                    vector_p4(v)

                @block.tensor
                def _(t_):
                    tensor_p4(t_)

            with contextlib.ExitStack() as stC:
                for nm, shape, dt in (
                    ("wob0", [128, MC * DBW], F32R),
                    ("bos", [2, D], F32R),
                ):
                    ctx[nm] = stC.enter_context(nc.sbuf_tensor(nm, shape, dt))
                for nm in ("wo0", "wo1"):
                    ctx[nm] = stC.enter_context(nc.psum_tensor(nm, [128, 512], F32))

                @block.sync
                def _(s):
                    sync_p5(s)

                @block.scalar
                def _(s):
                    scalar_p5(s)

                @block.vector
                def _(v):
                    vector_p5d(v)

                @block.tensor
                def _(t_):
                    tensor_p5(t_)

        with contextlib.ExitStack() as stD:
            for nm, shape, dt in (
                ("hsl", [128, 4 * 1024], F32R),
                ("gtp0", [128, D], F32R), ("gtp1", [128, D], F32R),
                ("utp0", [128, D], F32R), ("utp1", [128, D], F32R),
                ("dts", [128, 4 * D], F32R),
                ("gst", [128, 2048], F32),
            ):
                ctx[nm] = stD.enter_context(nc.sbuf_tensor(nm, shape, dt))
            for i in range(8):
                ctx[f"fp{i}"] = stD.enter_context(nc.psum_tensor(f"fp{i}", [128, 512], F32))

            @block.sync
            def _(s):
                sync_p6(s)

            @block.scalar
            def _(s):
                scalar_p6(s)

            @block.vector
            def _(v):
                vector_p6(v)

            @block.tensor
            def _(t_):
                tensor_p6(t_)

        @block.gpsimd
        def _(g):
            gpsimd_all(g)

    emit(False, {})
    for key in waits_seen:
        assert key in marks, f"wait on unmarked key {key}"

    with contextlib.ExitStack() as st:
        ctx = {}
        for nm in ("sC", "sX0", "sX1", "sGI0", "sGI1", "sWA", "sWB",
                   "sWQ0", "sWQ1", "sWO", "sG0", "sG1", "sU0", "sU1", "sDT",
                   "sKV", "sKR", "sGW0", "sGW1", "sY0", "sY1", "sYI",
                   "sCG", "sCR", "sCU", "sCD", "sPE", "sVE", "sSE"):
            ctx[nm] = st.enter_context(nc.semaphore(nm))
        for nm, shape, dt in (
            ("xnT", [128, MC * NTOK], F32R),
            ("gi0", [128, 1024], I32), ("gi1", [128, 1024], I32),
            ("gw0", [128, 1024], F32R), ("gw1", [128, 1024], F32R),
            ("gtm", [128, 1024], F32R),
            ("kvr", [128, H * 130], F32R),
            ("luts", [128, 16], F32), ("dcols", [128, NT], F32),
            ("bqcs", [128, H], F32), ("idents", [128, 128], F32),
            ("ones", [2, 512], F32R), ("onec", [128, 2], F32R),
            ("stats", [128, 56], F32), ("zs", [128, 4], F32), ("epsc", [128, 1], F32),
            ("ytw0", [128, 2048], F32), ("ytw1", [128, 2048], F32),
        ):
            ctx[nm] = st.enter_context(nc.sbuf_tensor(nm, shape, dt))
        with nc.Block() as block:
            ctx["block"] = block
            emit(True, ctx)
    return nc


def kernel(L0_attn_x, L0_attn_Wq, L0_attn_bq, L0_attn_Wk, L0_attn_bk,
           L0_attn_Wv, L0_attn_bv, L0_attn_Wo, L0_attn_bo,
           L0_ffn_lut, L0_ffn_gate_idx, L0_ffn_up_idx, L0_ffn_down_idx):
    x = np.ascontiguousarray(np.asarray(L0_attn_x, dtype=np.float32)[0])

    def pan_db(w):
        return np.ascontiguousarray(
            np.asarray(w, np.float32).T.reshape(MC, 128, NDB, DBW).transpose(2, 0, 1, 3))

    wq_pan = np.ascontiguousarray(
        np.asarray(L0_attn_Wq, np.float32).T.reshape(MC, 128, H, 128).transpose(2, 0, 1, 3))
    wk_pan = pan_db(L0_attn_Wk)
    wv_pan = pan_db(L0_attn_Wv)
    wo_pan = pan_db(L0_attn_Wo)
    bqc = np.ascontiguousarray(np.asarray(L0_attn_bq, np.float32).reshape(H, 128).T)
    bkv = np.zeros((2, 2 * D), np.float32)
    bkv[0] = np.concatenate([np.asarray(L0_attn_bk, np.float32),
                             np.asarray(L0_attn_bv, np.float32)])
    bo = np.zeros((2, D), np.float32)
    bo[0] = np.asarray(L0_attn_bo, np.float32)
    lut = np.asarray(L0_ffn_lut, np.float32)
    gidxT = np.ascontiguousarray(np.asarray(L0_ffn_gate_idx, np.int32).T)
    uidxT = np.ascontiguousarray(np.asarray(L0_ffn_up_idx, np.int32).T)
    didxT = np.zeros((8 * ICC * 128, D), np.int32)
    didxT[:INTER] = np.asarray(L0_ffn_down_idx, np.int32).T

    def idx_shard(idxT, c):
        out = np.zeros((ICC, 128, D), np.int32)
        for j in range(ICC):
            ic = ICC * c + j
            if ic < ICR:
                blk = idxT[:, ic * 128:(ic + 1) * 128]
                out[j] = blk.reshape(MC, 128, 128).transpose(1, 0, 2).reshape(128, D)
        return out

    decay = np.exp(-(np.arange(8192, dtype=np.float64) / 8192.0)).astype(np.float32)
    ident = np.eye(128, dtype=np.float32)
    _ones = np.ones((128, 512), np.float32)

    nc = build()
    in_maps = []
    for c in range(8):
        dc = np.ascontiguousarray(decay[NTOK * c: NTOK * (c + 1)].reshape(NT, 128).T)
        in_maps.append({
            "x": np.ascontiguousarray(x[NTOK * c: NTOK * (c + 1)]),
            "wq": wq_pan, "wk": wk_pan, "wv": wv_pan, "wo": wo_pan,
            "bqc": bqc, "bkv": bkv, "bo": bo,
            "gidx": idx_shard(gidxT, c), "uidx": idx_shard(uidxT, c),
            "didx": np.ascontiguousarray(
                didxT[768 * c: 768 * (c + 1)].reshape(ICC, 128, D)),
            "lut": lut, "dcol": dc, "ident": ident,
            "onesm": _ones,
        })
    res = run_bass_kernel_spmd(nc, in_maps, core_ids=list(range(8)))
    y = np.concatenate([np.asarray(res.results[c]["y"]) for c in range(8)], axis=0)
    return y.reshape(1, 8192, D).astype(np.float32)
